# version 06 (final)
# baseline (speedup 1.0000x reference)
import numpy as np
import concourse.bass as bass
import concourse.mybir as mybir
from concourse.bass_utils import run_bass_kernel_spmd
from concourse.alu_op_type import AluOpType as OP

F32 = mybir.dt.float32
BF16 = mybir.dt.bfloat16
AX = mybir.AxisListType.X
AF = mybir.ActivationFunctionType

B, T, D = 8, 4096, 1024
PT = 128
NTILE = T // PT
G = 4
NST = NTILE // G
EIN = 2336
ETOT = EIN + 40
ECH = 128
CONV_LO, CONV_HI = 1024, 2304
BIGM = 98304.0
EPS = 1e-5
NEGINF = -1e30

C_SUM, C_SS, C_M, C_T, C_MSQ, C_VE, C_LNV, C_R, C_RI = range(9)
C_MI = 9
C_RII = 13


def chunks(width, total):
    out, lo = [], 0
    while lo < total:
        out.append((lo, min(lo + width, total)))
        lo += width
    return out


class Chan:
    def __init__(self, step):
        self.step, self.n, self.sem = step, 0, None

    def bump(self):
        self.n += self.step
        return self.n


class KB:
    def __init__(self):
        self.nc = bass.Bass("TRN2", target_bir_lowering=False, detect_race_conditions=False)
        self.ops = {e: [] for e in ("tensor", "vector", "scalar", "gpsimd", "sync")}
        self.cTE, self.cV, self.cSC = Chan(1), Chan(1), Chan(1)
        self.cGP = Chan(1)
        self.dch = {n: Chan(16) for n in
                    ("W0", "W1", "U0", "U1", "U2", "U3", "ST", "PRE", "DBG")}
        self.dram = {}

    def add(self, eng, emit, waits=(), inc=None):
        ws = []
        for c, v in waits:
            assert v <= c.n, f"wait on future count {v} > {c.n}"
            if v > 0:
                ws.append((c, v))
        if inc is not None:
            inc.bump()

        def fn(e):
            for c, v in ws:
                e.wait_ge(c.sem, v)
            ins = emit(e)
            if inc is not None:
                ins.then_inc(inc.sem, inc.step)
        self.ops[eng].append(fn)

    def te(self, emit, waits=(), inc=False):
        self.add("tensor", emit, waits, self.cTE if inc else None)

    def v(self, emit, waits=(), inc=True):
        self.add("vector", emit, waits, self.cV if inc else None)

    def sc(self, emit, waits=(), inc=True):
        self.add("scalar", emit, waits, self.cSC if inc else None)

    def gp(self, emit, waits=(), inc=True):
        self.add("gpsimd", emit, waits, self.cGP if inc else None)

    def dma(self, q, ch, emit, waits=()):
        self.add(q, emit, waits, self.dch[ch])

    def vdrain(self):
        self.add("vector", lambda e: e.drain())

    def scdrain(self):
        self.add("scalar", lambda e: e.drain())


def bc_h(ap2d, H, inner):
    P = ap2d.shape[0]
    return ap2d.rearrange("p (h o) -> p h o", o=1).broadcast_to((P, H, inner))


def bc_g(ap2d, gdim, W):
    P = ap2d.shape[0]
    return ap2d.rearrange("p (o w) -> p o w", o=1).broadcast_to((P, gdim, W))


def g3(ap, g=G):
    return ap.rearrange("p (g w) -> p g w", g=g)


def build_core(dumps=(), nst=NST):
    k = KB()
    nc = k.nc

    def din(name, shape, dt=F32):
        k.dram[name] = nc.dram_tensor(name, list(shape), dt, kind="ExternalInput").ap()
        return k.dram[name]

    u_hbm = din("u", (T, D))
    wcat = din("wcat", (128, 8, ETOT))
    wo = din("wo", (128, 8, D))
    wffn = din("wffn", (D, D))
    whdt = din("whdt", (32, 40))
    wfm_h = din("wfm", (128, 10, 4))
    cbfm_h = din("cbfm", (128, 10))
    dbias = din("dbias", (128, 1))
    gam = din("gam", (128, 16))
    alog = din("alog", (128, 16))
    dvec = din("dvec", (128, 16))
    nww = din("nww", (128, D))
    fbias = din("fbias", (1, D))
    ident = din("ident", (128, 128))
    cumm = din("cumm", (128, 128))
    ust = din("ust", (128, 2048), BF16)
    ibf = din("ibf", (128, 128), BF16)
    i16 = din("i16", (16, 16))
    ones16 = din("ones16", (16, 128))
    ones1 = din("ones1", (1, 128))
    ones128 = din("ones128", (128, 128))
    ninvt = din("ninvt", (128, NTILE))

    out_hbm = nc.dram_tensor("out", [T, D], F32, kind="ExternalOutput").ap()
    wdqT = nc.dram_tensor("wdqT", [128, 8, D], F32, kind="Internal").ap()
    dbg = {}
    for nm, shape in dumps:
        dbg[nm] = nc.dram_tensor("dbg_" + nm, list(shape), F32,
                                 kind="ExternalOutput").ap()

    from contextlib import ExitStack
    with ExitStack() as _st:
        def sb(name, shape, dt=F32):
            return _st.enter_context(nc.sbuf_tensor(name, list(shape), dt))

        def pst(name, shape):
            return _st.enter_context(nc.psum_tensor(name, list(shape), F32))

        def sem(name):
            return _st.enter_context(nc.semaphore(name))

        strm = sb("strm", [128, 2, 8, ECH], F32)
        x_fm = sb("x_fm", [128, G, 8, PT], F32)
        u_st = sb("u_st", [128, G, D], F32)
        silz = sb("silz", [128, G, D], F32)
        xbcf = sb("xbcf", [128, 2, 515], F32)
        cvout = sb("cvout", [128, 512], F32)
        cv2 = sb("cv2", [128, 512], F32)
        sgbuf = sb("sgbuf", [128, 512], F32)
        bcfm = sb("bcfm", [128, 2, 512], F32)
        xtm = sb("xtm", [128, G, 1024], F32)
        btm = sb("btm", [128, 128], F32)
        halo_sv = sb("halo_sv", [128, 10, 3], F32)
        dtall = sb("dtall", [128, G, 32], F32)
        hbb = sb("hbb", [128, G, 40], F32)
        hu_t = sb("hu_t", [128, 40], F32)
        rhc = sb("rhc", [128, 40], F32)
        hwork = sb("hwork", [128, G, 28], F32)
        mskw = sb("mskw", [128, G, 28], F32)
        dtsel = sb("dtsel", [128, G, 16], F32)
        dttmp = sb("dttmp", [128, G, 16], F32)
        dtsp = sb("dtsp", [128, G, 16], F32)
        lndt = sb("lndt", [128, G, 16], F32)
        dta = sb("dta", [128, G, 16], F32)
        colg = sb("colg", [128, G], F32)
        cols = sb("cols", [128, 24], F32)
        cumsb = sb("cumsb", [128, 16], F32)
        clb = sb("clb", [128, 16], F32)
        edt = sb("edt", [128, 16], F32)
        lamb = sb("lamb", [128, 16], F32)
        fexp = sb("fexp", [128, 16], F32)
        cumst = sb("cumst", [16, 128], F32)
        mskt = sb("mskt", [16, 1024], F32)
        gbuf = sb("gbuf", [128, 2048], F32)
        cbsb = sb("cbsb", [128, 128], F32)
        dtfm = sb("dtfm", [32, 128], F32)
        xw = sb("xw", [128, 1024], F32)
        state = sb("state", [128, 1024], F32)
        ybuf = sb("ybuf", [128, 1024], F32)
        scr1 = sb("scr1", [128, 1280], F32)
        scr4 = sb("scr4", [128, 1024], F32)
        yzfm = sb("yzfm", [128, G, 8, PT], F32)
        xdfm = sb("xdfm", [128, G, 8, PT], F32)
        whdt_sb = sb("whdt_sb", [32, 40], F32)
        wsh = sb("wsh", [128, 40], F32)
        wfm = sb("wfm_s", [128, 10, 4], F32)
        cbfm = sb("cbfm_s", [128, 10], F32)
        cbn = sb("cbn", [128, 10], F32)
        dbias_sb = sb("dbias_sb", [128, 1], F32)
        gam_sb = sb("gam_sb", [128, 16], F32)
        abc = sb("abc", [128, 16], F32)
        dvec_sb = sb("dvec_sb", [128, 16], F32)
        nww_sb = sb("nww_sb", [128, D], F32)
        fb_sb = sb("fb_sb", [1, D], F32)
        fbw = sb("fbw", [128, D], F32)
        ident_sb = sb("ident_sb", [128, 128], F32)
        cumm_sb = sb("cumm_sb", [128, 128], F32)
        ust_sb = sb("ust_sb", [128, 2048], BF16)
        ibf_sb = sb("ibf_sb", [128, 128], BF16)
        i16_sb = sb("i16_sb", [16, 16], F32)
        o16_sb = sb("o16_sb", [16, 128], F32)
        o1_sb = sb("o1_sb", [1, 128], F32)
        o128_sb = sb("o128_sb", [128, 128], F32)
        ninvt_sb = sb("ninvt_sb", [128, NTILE], F32)
        s8 = sb("s8", [128, 32], F32)
        rq8 = sb("rq8", [128, 32], F32)
        bmx = sb("bmx", [128, 32], F32)
        ps_a = pst("ps_a", [128, 1024])
        ps_b = pst("ps_b", [128, 1024])
        ps_x = pst("ps_x", [128, 2048])
        sTE = sem("sTE")
        sV = sem("sV")
        sSC = sem("sSC")
        sGP = sem("sGP")
        block = _st.enter_context(nc.Block())

        k.cTE.sem, k.cV.sem, k.cSC.sem, k.cGP.sem = sTE, sV, sSC, sGP
        for nmc, chc in k.dch.items():
            chc.sem = _st.enter_context(nc.semaphore("sD" + nmc))
        cTE, cV, cSC, cGP = k.cTE, k.cV, k.cSC, k.cGP
        cW = [k.dch["W0"], k.dch["W1"]]
        cU = [k.dch["U0"], k.dch["U1"], k.dch["U2"], k.dch["U3"]]
        cST, cPRE, cDBG = k.dch["ST"], k.dch["PRE"], k.dch["DBG"]
        TRASH = scr4

        def dump(name, ap, waits):
            if name in dbg:
                k.dma("sync", "DBG",
                      lambda e, a=ap, d=dbg[name]: e.dma_start(out=d, in_=a),
                      waits=waits)

        for dst, src in [
            (whdt_sb, whdt), (wfm, wfm_h), (cbfm, cbfm_h), (dbias_sb, dbias),
            (gam_sb, gam), (abc, alog), (dvec_sb, dvec), (nww_sb, nww),
            (fb_sb, fbias), (ident_sb, ident), (cumm_sb, cumm), (ust_sb, ust),
            (ibf_sb, ibf), (i16_sb, i16), (o16_sb, ones16), (o1_sb, ones1),
            (o128_sb, ones128), (ninvt_sb, ninvt),
        ]:
            k.dma("sync", "PRE", lambda e, d=dst, s=src: e.dma_start(out=d[:], in_=s))
        consts_w = None

        k.dma("sync", "PRE", lambda e: e.dma_start(
            out=strm[:, 0, :, 0:40], in_=wcat[:, :, EIN:ETOT]))
        consts_w = cPRE.n
        k.sc(lambda e: e.activation(abc[:], abc[:], AF.Exp), waits=[(cPRE, consts_w)])
        k.v(lambda e: e.tensor_scalar(abc[:], abc[:], -1.0, None, OP.mult),
            waits=[(cSC, cSC.n)])
        k.v(lambda e: e.tensor_scalar(cbn[:], cbfm[:], -1.0, None, OP.mult),
            waits=[(cPRE, consts_w)])
        k.te(lambda e: e.matmul(ps_b[:, 0:512], o1_sb[:], fb_sb[0:1, 0:512],
                                start=True, stop=True), waits=[(cPRE, consts_w)])
        k.te(lambda e: e.matmul(ps_b[:, 512:1024], o1_sb[:], fb_sb[0:1, 512:1024],
                                start=True, stop=True), inc=True)
        k.v(lambda e: e.tensor_copy(fbw[:], ps_b[:]), waits=[(cTE, cTE.n)])
        fbw_v = cV.n
        wsh_w = consts_w
        for c in range(8):
            k.te(lambda e, c=c: e.matmul(ps_b[:, 0:40], o128_sb[:],
                                         strm[:, 0, c, 0:40],
                                         start=(c == 0), stop=(c == 7)),
                 waits=[(cPRE, wsh_w), (cV, fbw_v)] if c == 0 else [],
                 inc=(c == 7))
        wsh_te = cTE.n
        k.v(lambda e: e.tensor_copy(wsh[:], ps_b[:, 0:40]), waits=[(cTE, cTE.n)])

        def quant_core(src, qa, qb, qc, pre_waits):
            qau = qa[:, 1024:2048]
            k.v(lambda e: e.tensor_tensor(
                src[:, 0:1024].rearrange("p (b kk) -> p b kk", kk=32),
                src[:, 0:1024].rearrange("p (b kk) -> p b kk", kk=32),
                bc_h(rq8[:], 32, 32), OP.mult), waits=pre_waits)
            uv = cV.n
            k.sc(lambda e: e.activation(qc[:, 0:1024], src[:, 0:1024], AF.Abs),
                 waits=[(cV, uv)])
            absc = cSC.n
            k.v(lambda e: e.tensor_scalar(qau, qc[:, 0:1024], 16.0, None,
                                          OP.is_ge), waits=[(cSC, absc)])
            k.v(lambda e: e.tensor_scalar(qb[:, 0:1024], qc[:, 0:1024], 32.0, None,
                                          OP.is_ge))
            k.vdrain()
            k.v(lambda e: e.tensor_scalar(qc[:, 0:1024], qb[:, 0:1024], -0.25, 1.0,
                                          OP.mult, OP.add))
            k.vdrain()
            k.v(lambda e: e.tensor_scalar(qb[:, 0:1024], qb[:, 0:1024], 2.0, 1.0,
                                          OP.mult, OP.add))
            k.v(lambda e: e.scalar_tensor_tensor(qc[:, 0:1024], qau, -0.5,
                                                 qc[:, 0:1024], OP.mult, OP.add))
            k.v(lambda e: e.scalar_tensor_tensor(qb[:, 0:1024], qau, 1.0,
                                                 qb[:, 0:1024], OP.mult, OP.add))
            H = 512
            for hf in range(2):
                k.v(lambda e, hf=hf: e.tensor_tensor(
                    qc[:, hf * H:(hf + 1) * H], src[:, hf * H:(hf + 1) * H],
                    qc[:, hf * H:(hf + 1) * H], OP.mult))
            for hf in range(2):
                k.v(lambda e, hf=hf: e.tensor_scalar(
                    qau[:, hf * H:(hf + 1) * H], qc[:, hf * H:(hf + 1) * H],
                    16.5, None, OP.add))
            for si, stp in enumerate((32.0, 16.0, 8.0, 4.0, 2.0, 1.0)):
                rb = qau if si == 0 else qc
                for hf in range(2):
                    k.v(lambda e, hf=hf, stp=stp, rb=rb: e.tensor_scalar(
                        src[:, hf * H:(hf + 1) * H], rb[:, hf * H:(hf + 1) * H],
                        stp, None, OP.is_ge))
                for hf in range(2):
                    k.v(lambda e, hf=hf, stp=stp, rb=rb: e.scalar_tensor_tensor(
                        qc[:, hf * H:(hf + 1) * H], src[:, hf * H:(hf + 1) * H],
                        -stp, rb[:, hf * H:(hf + 1) * H], OP.mult, OP.add))
            for hf in range(2):
                k.v(lambda e, hf=hf: e.tensor_tensor(
                    qau[:, hf * H:(hf + 1) * H], qau[:, hf * H:(hf + 1) * H],
                    qc[:, hf * H:(hf + 1) * H], OP.subtract))
            for hf in range(2):
                k.v(lambda e, hf=hf: e.scalar_tensor_tensor(
                    qau[:, hf * H:(hf + 1) * H], qau[:, hf * H:(hf + 1) * H],
                    -16.0, qb[:, hf * H:(hf + 1) * H], OP.add, OP.mult))
            for hf in range(2):
                k.v(lambda e, hf=hf: e.tensor_tensor(
                    src[:, hf * H:(hf + 1) * H].rearrange(
                        "p (b kk) -> p b kk", kk=32),
                    qau[:, hf * H:(hf + 1) * H].rearrange(
                        "p (b kk) -> p b kk", kk=32),
                    bc_h(s8[:, hf * 16:(hf + 1) * 16], 16, 32), OP.mult))

        for c in range(8):
            k.dma("sync", "PRE", lambda e, c=c: e.dma_start(
                out=scr1[:, 0:1024], in_=wffn[c * 128:(c + 1) * 128, :]),
                waits=[(cV, cV.n), (cSC, cSC.n)])
            wld = cPRE.n
            k.v(lambda e: e.tensor_reduce(
                bmx[:], scr1[:, 0:1024].rearrange("p (b kk) -> p b kk", kk=32),
                AX, OP.max, apply_absolute_value=True), waits=[(cPRE, wld)])
            k.vdrain()
            k.v(lambda e: e.tensor_scalar(s8[:], bmx[:], 1.0 / 7.5, 1e-30,
                                          OP.mult, OP.max))
            k.vdrain()
            k.v(lambda e: e.tensor_scalar(rq8[:], s8[:], 0.125, None, OP.mult))
            rv = cV.n
            k.sc(lambda e: e.activation(rq8[:], rq8[:], AF.Ln), waits=[(cV, rv)])
            k.scdrain()
            k.sc(lambda e: e.activation(rq8[:], rq8[:], AF.Exp, scale=-1.0))
            rsc = cSC.n
            k.v(lambda e: e.tensor_scalar(s8[:], s8[:], 0.125, None, OP.mult),
                waits=[(cSC, rsc)])
            quant_core(scr1, gbuf, ybuf, TRASH, [(cV, cV.n)])
            wd_v = cV.n
            for cd in range(8):
                w = []
                if cd == 0:
                    w = [(cV, wd_v)]
                if cd == 4:
                    w = [(cSC, cSC.n)]
                k.te(lambda e, cd=cd: e.transpose(
                    ps_x[:, 1536 + (cd % 4) * 128:1536 + (cd % 4) * 128 + 128],
                    scr1[:, cd * 128:(cd + 1) * 128], ident_sb[:]),
                    waits=w, inc=(cd % 4 == 3))
                if cd % 4 == 3:
                    te_n = cTE.n
                    k.sc(lambda e, cd=cd: e.copy(
                        gbuf[:, (cd - 3) * 128:(cd + 1) * 128], ps_x[:, 1536:2048]),
                        waits=[(cTE, te_n)])
            k.dma("sync", "PRE", lambda e, c=c: e.dma_start(
                out=wdqT[:, :, c * 128:(c + 1) * 128],
                in_=gbuf[:, 0:1024].rearrange("p (q e) -> p q e", q=8)),
                waits=[(cSC, cSC.n)])
        prep_dw = cPRE.n

        k.gp(lambda e: e.memset(state[:], 0.0))
        k.gp(lambda e: e.memset(rhc[:], 0.0))
        k.gp(lambda e: e.memset(halo_sv[:], 0.0))
        k.gp(lambda e: e.memset(dttmp[:], 0.0))
        init_gp = cGP.n

        inch = chunks(ECH, ETOT)
        ech8 = chunks(ECH, D)
        strm_gate = {0: wsh_te, 1: 0}
        prev_tail_v = cV.n
        prev_stageC = [(cV, cV.n), (cSC, cSC.n)]
        prev_store_st = cST.n

        for sti in range(nst):
            t0 = sti * G
            xfm_sc = []
            for i in range(G):
                tile = t0 + i
                k.dma("gpsimd", "U%d" % i, lambda e, tile=tile, i=i: e.dma_start(
                    out=u_st[:, i, :], in_=u_hbm[tile * PT:(tile + 1) * PT, :]),
                    waits=[(cGP, init_gp), (cST, prev_store_st)] +
                          (prev_stageC if sti > 0 else []))
                uld = cU[i].n
                k.sc(lambda e, i=i: e.activation(
                    TRASH[:, 0:1024], u_st[:, i, :], AF.Identity,
                    accum_out=cols[:, C_SUM:C_SUM + 1]), waits=[(cU[i], uld)])
                k.sc(lambda e, i=i: e.activation(
                    TRASH[:, 0:1024], u_st[:, i, :], AF.Square,
                    accum_out=cols[:, C_SS:C_SS + 1]))
                stat_sc = cSC.n
                k.v(lambda e: e.tensor_scalar(
                    cols[:, C_MSQ:C_MSQ + 1], cols[:, C_SUM:C_SUM + 1],
                    cols[:, C_SUM:C_SUM + 1], 1.0 / (D * D), OP.mult, OP.mult),
                    waits=[(cSC, stat_sc)])
                k.v(lambda e: e.tensor_scalar(
                    cols[:, C_T:C_T + 1], cols[:, C_SS:C_SS + 1], 1.0 / D, EPS,
                    OP.mult, OP.add))
                k.v(lambda e: e.tensor_scalar(
                    cols[:, C_M:C_M + 1], cols[:, C_SUM:C_SUM + 1], 1.0 / D, None,
                    OP.mult))
                k.v(lambda e: e.tensor_tensor(
                    cols[:, C_VE:C_VE + 1], cols[:, C_T:C_T + 1],
                    cols[:, C_MSQ:C_MSQ + 1], OP.subtract))
                ve_v = cV.n
                k.v(lambda e, i=i: e.tensor_copy(cols[:, C_MI + i:C_MI + i + 1],
                                                 cols[:, C_M:C_M + 1]))
                k.sc(lambda e: e.activation(cols[:, C_LNV:C_LNV + 1],
                                            cols[:, C_VE:C_VE + 1], AF.Ln),
                     waits=[(cV, ve_v)])
                k.scdrain()
                k.sc(lambda e: e.activation(cols[:, C_R:C_R + 1],
                                            cols[:, C_LNV:C_LNV + 1], AF.Exp,
                                            scale=-0.5))
                k.sc(lambda e: e.activation(cols[:, C_RI:C_RI + 1],
                                            cols[:, C_LNV:C_LNV + 1], AF.Exp,
                                            scale=0.5))
                r_sc = cSC.n
                k.v(lambda e, i=i: e.tensor_copy(cols[:, C_RII + i:C_RII + i + 1],
                                                 cols[:, C_RI:C_RI + 1]),
                    waits=[(cSC, r_sc)])
                k.v(lambda e, i=i: e.tensor_scalar(
                    scr1[:, 0:1024], u_st[:, i, :], cols[:, C_M:C_M + 1],
                    cols[:, C_R:C_R + 1], OP.subtract, OP.mult),
                    waits=[(cTE, cTE.n)])
                xln_v = cV.n
                dump("xln%d" % tile, scr1[:, 0:1024], [(cV, xln_v)])
                for gch in range(2):
                    for c4 in range(4):
                        cc = gch * 4 + c4
                        w = []
                        if cc == 0:
                            w = [(cV, xln_v), (cSC, cSC.n)]
                        if cc == 4:
                            w = [(cSC, cSC.n)]
                        k.te(lambda e, cc=cc, c4=c4: e.transpose(
                            ps_x[:, 1024 + c4 * 128:1024 + (c4 + 1) * 128],
                            scr1[:, cc * 128:(cc + 1) * 128], ident_sb[:]),
                            waits=w, inc=(c4 == 3))
                    te_n = cTE.n
                    k.sc(lambda e, i=i, gch=gch: e.copy(
                        x_fm[:, i, gch * 4:(gch + 1) * 4, :].rearrange(
                            "p c t -> p (c t)"),
                        ps_x[:, 1024:1536]), waits=[(cTE, te_n)])
                xfm_sc.append(cSC.n)
            xfm_all = xfm_sc[-1]

            evac_gate = {}
            conv_tapv = {}
            for ec, (lo, hi) in enumerate(inch):
                wd_ = hi - lo
                bufi = ec % 2
                k.dma("sync", "W%d" % bufi, lambda e, lo=lo, hi=hi, bufi=bufi:
                      e.dma_start(
                          out=strm[:, bufi, :, 0:hi - lo],
                          in_=wcat[:, :, lo:hi]),
                      waits=[(cTE, strm_gate[bufi])] +
                            ([(cPRE, prep_dw)] if sti == 0 and ec < 2 else []))
                wch = cW[bufi].n
                ps = ps_a if ec % 2 == 0 else ps_b
                if CONV_LO <= lo and hi <= CONV_HI:
                    cch = (lo - CONV_LO) // 128
                    for c8 in range(8):
                        waits = []
                        if c8 == 0:
                            waits = [(cW[bufi], wch), (cSC, xfm_all)]
                            waits += evac_gate.get(ec - 2, prev_stageC)
                        k.te(lambda e, c8=c8, ps=ps, bufi=bufi: e.matmul(
                            ps[:, 0:512], strm[:, bufi, c8, :],
                            x_fm[:, :, c8, :], start=(c8 == 0), stop=(c8 == 7)),
                            waits=waits, inc=(c8 == 7))
                    mm_te = cTE.n
                    strm_gate[bufi] = mm_te
                    xb = xbcf[:, ec % 2, :]
                    k.v(lambda e, cch=cch, xb=xb: e.tensor_copy(
                        xb[:, 0:3], halo_sv[:, cch, :]))
                    k.sc(lambda e, ps=ps, xb=xb: e.copy(xb[:, 3:515], ps[:, 0:512]),
                         waits=[(cTE, mm_te), (cV, conv_tapv.get(ec - 2, 0))])
                    xb_sc = cSC.n
                    k.v(lambda e, cch=cch, xb=xb: e.tensor_scalar(
                        cv2[:], xb[:, 3:515], wfm[:, cch, 3:4], None, OP.mult),
                        waits=[(cSC, xb_sc)])
                    k.v(lambda e, cch=cch, xb=xb: e.tensor_scalar(
                        cvout[:], xb[:, 2:514], wfm[:, cch, 2:3], None, OP.mult))
                    k.v(lambda e, cch=cch, xb=xb: e.scalar_tensor_tensor(
                        cv2[:], xb[:, 1:513], wfm[:, cch, 1:2],
                        cv2[:], OP.mult, OP.add))
                    k.v(lambda e, cch=cch, xb=xb: e.scalar_tensor_tensor(
                        cvout[:], xb[:, 0:512], wfm[:, cch, 0:1],
                        cvout[:], OP.mult, OP.add))
                    k.v(lambda e, cch=cch, xb=xb: e.tensor_copy(
                        halo_sv[:, cch, :], xb[:, 512:515]))
                    k.v(lambda e, cch=cch: e.scalar_tensor_tensor(
                        cvout[:], cvout[:], cbfm[:, cch:cch + 1], cv2[:],
                        OP.add, OP.add))
                    conv_tapv[ec] = cV.n
                    k.sc(lambda e: e.activation(
                        sgbuf[:], cvout[:], AF.Exp, scale=-1.0),
                        waits=[(cV, cV.n)])
                    k.scdrain()
                    k.sc(lambda e: e.activation(sgbuf[:], sgbuf[:], AF.Ln, bias=1.0))
                    k.scdrain()
                    k.sc(lambda e: e.activation(sgbuf[:], sgbuf[:], AF.Exp,
                                                scale=-1.0))
                    sg_sc = cSC.n
                    k.v(lambda e: e.tensor_tensor(sgbuf[:], cvout[:], sgbuf[:],
                                                  OP.mult), waits=[(cSC, sg_sc)])
                    cvf_v = cV.n
                    if cch <= 7:
                        for i in range(G):
                            w = []
                            if i == 0:
                                w = [(cV, cvf_v), (cSC, cSC.n)]
                            k.te(lambda e, i=i: e.transpose(
                                ps_x[:, 1024 + i * 128:1024 + (i + 1) * 128],
                                sgbuf[:, i * 128:(i + 1) * 128], ident_sb[:]),
                                waits=w, inc=(i == 3))
                        tr_te = cTE.n
                        for i in range(G):
                            k.sc(lambda e, i=i, cch=cch: e.copy(
                                xtm[:, i, cch * 128:(cch + 1) * 128],
                                ps_x[:, 1024 + i * 128:1024 + (i + 1) * 128]),
                                waits=[(cTE, tr_te)] if i == 0 else [])
                    else:
                        k.vdrain()
                        k.v(lambda e, cch=cch: e.tensor_copy(
                            bcfm[:, cch - 8, :], sgbuf[:]))
                    evac_gate[ec] = [(cSC, xb_sc)]
                    continue
                for i in range(G):
                    for c in range(8):
                        waits = []
                        if i == 0 and c == 0:
                            waits = [(cW[bufi], wch), (cSC, xfm_all)]
                            waits += evac_gate.get(ec - 2, prev_stageC)
                        k.te(lambda e, i=i, c=c, ps=ps, wd_=wd_, bufi=bufi: e.matmul(
                            ps[:, i * 256:i * 256 + wd_], x_fm[:, i, c, :],
                            strm[:, bufi, c, 0:wd_], start=(c == 0), stop=(c == 7)),
                            waits=waits, inc=(i == G - 1 and c == 7))
                mm_te = cTE.n
                strm_gate[bufi] = mm_te
                ev = []
                for i in range(G):
                    tile = t0 + i
                    if hi <= CONV_LO:
                        k.sc(lambda e, i=i, ps=ps, lo=lo, wd_=wd_: e.activation(
                            silz[:, i, lo:lo + wd_], ps[:, i * 256:i * 256 + wd_],
                            AF.Silu), waits=[(cTE, mm_te)])
                        ev.append((cSC, cSC.n))
                    else:
                        k.v(lambda e, i=i, ps=ps: e.tensor_scalar(
                            hu_t[:], ps[:, i * 256 + 32:i * 256 + 72],
                            cols[:, C_RII + i:C_RII + i + 1], None, OP.mult),
                            waits=[(cTE, mm_te), (cV, prev_tail_v)])
                        k.v(lambda e, i=i, ps=ps: e.tensor_copy(
                            dtall[:, i, :], ps[:, i * 256 + 0:i * 256 + 32]))
                        k.v(lambda e, i=i: e.scalar_tensor_tensor(
                            hu_t[:], wsh[:], cols[:, C_MI + i:C_MI + i + 1],
                            hu_t[:], OP.mult, OP.add))
                        hu_v = cV.n
                        k.te(lambda e: e.matmul(ps_x[:, 1024:1064], cumm_sb[:],
                                                hu_t[:], start=True, stop=False),
                             waits=[(cV, hu_v), (cSC, cSC.n)])
                        k.te(lambda e: e.matmul(ps_x[:, 1024:1064], o1_sb[:],
                                                rhc[0:1, :], start=False, stop=True),
                             inc=True)
                        cum_te = cTE.n
                        k.te(lambda e: e.matmul(ps_x[:, 1536:1576], o128_sb[:],
                                                hu_t[:], start=True, stop=True),
                             inc=True)
                        colsum_te = cTE.n
                        k.te(lambda e, i=i: e.transpose(
                            ps_x[0:32, 1152:1280], dtall[:, i, :], ident_sb[:]),
                            inc=True)
                        dtt_te = cTE.n
                        k.v(lambda e: e.tensor_copy(dtfm[:], ps_x[0:32, 1152:1280]),
                            waits=[(cTE, dtt_te)])
                        dtfm_v = cV.n
                        k.te(lambda e: e.matmul(ps_x[:, 1312:1352], dtfm[:],
                                                whdt_sb[:], start=True, stop=True),
                             waits=[(cV, dtfm_v)], inc=True)
                        whdt_te = cTE.n
                        k.v(lambda e, i=i, tile=tile: e.scalar_tensor_tensor(
                            hbb[:, i, :], ps_x[:, 1024:1064],
                            ninvt_sb[:, tile:tile + 1], hu_t[:], OP.mult, OP.add),
                            waits=[(cTE, whdt_te)])
                        k.v(lambda e: e.tensor_tensor(
                            rhc[:], rhc[:], ps_x[:, 1536:1576], OP.add),
                            waits=[(cTE, colsum_te)])
                        k.v(lambda e, i=i: e.tensor_tensor(
                            hbb[:, i, :], hbb[:, i, :], ps_x[:, 1312:1352], OP.add))
                        prev_tail_v = cV.n
                        ev.append((cV, cV.n))
                evac_gate[ec] = ev
            inproj_v, inproj_sc = cV.n, cSC.n
            dump("silz%d" % t0, silz[:, 0, :], [(cSC, inproj_sc)])
            dump("dtall%d" % t0, dtall[:, 0, :], [(cV, inproj_v)])
            dump("hb%d" % t0, hbb[:, 0, :], [(cV, inproj_v)])

            k.v(lambda e: e.tensor_copy(hwork[:][:, 0:2, :], hbb[:][:, 0:2, 0:28]),
                waits=[(cV, inproj_v), (cSC, inproj_sc)])
            k.v(lambda e: e.tensor_copy(hwork[:][:, 2:4, :], hbb[:][:, 2:4, 0:28]))
            GH = G // 2
            for j in range(12):
                for hf in range(2):
                    sl = slice(hf * GH, (hf + 1) * GH)
                    k.v(lambda e, sl=sl, hf=hf: e.tensor_reduce(
                        colg[:, sl], hwork[:][:, sl, :], AX, OP.max))
                for hf in range(2):
                    sl = slice(hf * GH, (hf + 1) * GH)
                    k.v(lambda e, sl=sl: e.tensor_tensor(
                        mskw[:][:, sl, :], hwork[:][:, sl, :],
                        bc_h(colg[:, sl], GH, 28), OP.is_equal))
                for hf in range(2):
                    sl = slice(hf * GH, (hf + 1) * GH)
                    k.v(lambda e, sl=sl, hf=hf: e.tensor_tensor(
                        TRASH[:, hf * 56:(hf + 1) * 56].rearrange(
                            "p (g w) -> p g w", g=GH),
                        mskw[:][:, sl, :], dtall[:][:, sl, 0:28], OP.mult))
                for hf in range(2):
                    sl = slice(hf * GH, (hf + 1) * GH)
                    k.v(lambda e, sl=sl, hf=hf, j=j: e.tensor_reduce(
                        dtsel[:][:, sl, j:j + 1],
                        TRASH[:, hf * 56:(hf + 1) * 56].rearrange(
                            "p (g w) -> p g w", g=GH), AX, OP.add))
                if j < 11:
                    for hf in range(2):
                        sl = slice(hf * GH, (hf + 1) * GH)
                        k.v(lambda e, sl=sl: e.scalar_tensor_tensor(
                            hwork[:][:, sl, :], mskw[:][:, sl, :], NEGINF,
                            hwork[:][:, sl, :], OP.mult, OP.add))
            k.v(lambda e: e.tensor_copy(dttmp[:][:, :, 0:12],
                                        hbb[:][:, :, 28:40]))
            k.v(lambda e: e.tensor_copy(dtsel[:][:, :, 12:16],
                                        dtall[:][:, :, 28:32]))
            k.v(lambda e: e.tensor_tensor(dttmp[:][:, 0:2, :], dttmp[:][:, 0:2, :],
                                          bc_g(gam_sb[:], GH, 16), OP.mult))
            k.v(lambda e: e.tensor_tensor(dttmp[:][:, 2:4, :], dttmp[:][:, 2:4, :],
                                          bc_g(gam_sb[:], GH, 16), OP.mult))
            k.v(lambda e: e.scalar_tensor_tensor(
                dttmp[:][:, 0:2, :], dttmp[:][:, 0:2, :], dbias_sb[:, 0:1],
                dtsel[:][:, 0:2, :], OP.add, OP.add))
            k.v(lambda e: e.scalar_tensor_tensor(
                dttmp[:][:, 2:4, :], dttmp[:][:, 2:4, :], dbias_sb[:, 0:1],
                dtsel[:][:, 2:4, :], OP.add, OP.add))
            dt2_v = cV.n
            k.sc(lambda e: e.activation(dtsp[:], dttmp[:], AF.Exp),
                 waits=[(cV, dt2_v)])
            k.scdrain()
            k.sc(lambda e: e.activation(dtsp[:], dtsp[:], AF.Ln, bias=1.0))
            k.scdrain()
            k.sc(lambda e: e.activation(lndt[:], dtsp[:], AF.Ln))
            sp_sc = cSC.n
            k.v(lambda e: e.tensor_tensor(dta[:][:, 0:2, :], dtsp[:][:, 0:2, :],
                                          bc_g(abc[:], GH, 16), OP.mult),
                waits=[(cSC, sp_sc)])
            k.v(lambda e: e.tensor_tensor(dta[:][:, 2:4, :], dtsp[:][:, 2:4, :],
                                          bc_g(abc[:], GH, 16), OP.mult))
            k.v(lambda e: e.tensor_scalar(dttmp[:], dttmp[:], 0.0, None, OP.mult))
            topk_v = cV.n
            dump("dtsp%d" % t0, dtsp[:], [(cSC, sp_sc)])

            for i in range(G):
                tile = t0 + i
                k.te(lambda e, i=i: e.matmul(ps_b[:, 256:272], cumm_sb[:],
                                             dta[:][:, i, :],
                                             start=True, stop=True),
                     waits=[(cV, cV.n)], inc=True)
                k.te(lambda e, i=i: e.matmul(ps_b[:, 512:528], o128_sb[:],
                                             dta[:][:, i, :],
                                             start=True, stop=True), inc=True)
                cums_te = cTE.n
                k.v(lambda e: e.tensor_copy(cumsb[:], ps_b[:, 256:272]),
                    waits=[(cTE, cums_te)])
                cumsb_v = cV.n
                k.v(lambda e, i=i: e.tensor_tensor(clb[:], ps_b[:, 256:272],
                                                   lndt[:][:, i, :],
                                                   OP.subtract))
                clb_v = cV.n
                k.te(lambda e: e.transpose(ps_b[0:16, 288:416], cumsb[:],
                                           ident_sb[:]),
                     waits=[(cV, clb_v)], inc=True)
                ct_te = cTE.n
                k.te(lambda e, i=i: e.transpose(
                    ps_b[:, 0:128], bcfm[:, 0, i * 128:(i + 1) * 128],
                    ident_sb[:]), inc=True)
                bct_te = cTE.n
                k.v(lambda e: e.tensor_copy(btm[:], ps_b[:, 0:128]),
                    waits=[(cTE, bct_te)])
                k.v(lambda e: e.tensor_tensor(edt[:], ps_b[:, 512:528], clb[:],
                                              OP.subtract))
                sml_v = cV.n
                k.sc(lambda e: e.activation(edt[:], edt[:], AF.Exp),
                     waits=[(cV, sml_v)])
                k.sc(lambda e: e.activation(lamb[:], ps_b[:, 512:528], AF.Exp))
                k.sc(lambda e: e.activation(fexp[:], cumsb[:], AF.Exp))
                sml_sc = cSC.n
                dump("cums%d" % tile, cumsb[:], [(cV, cumsb_v)])
                dump("edt%d" % tile, edt[:], [(cSC, sml_sc)])
                dump("fexp%d" % tile, fexp[:], [(cSC, sml_sc)])
                k.v(lambda e: e.tensor_copy(cumst[:], ps_b[0:16, 288:416]),
                    waits=[(cTE, bct_te)])
                cumst_v = cV.n
                bcf_v = cV.n
                dump("cumst%d" % tile, cumst[:], [(cV, cumst_v)])
                k.v(lambda e, i=i: e.tensor_tensor(
                    xw[:].rearrange("p (h d) -> p h d", h=16),
                    xtm[:, i, :].rearrange("p (h d) -> p h d", h=16),
                    bc_h(edt[:], 16, 64), OP.mult), waits=[(cSC, sml_sc)])
                xw_v = cV.n
                half_te = 0
                for hf in range(2):
                    k.v(lambda e, hf=hf: e.tensor_tensor(
                        mskt[:].rearrange("p (h t) -> p h t", h=8),
                        cumst[:].rearrange("p (o t) -> p o t", o=1).broadcast_to(
                            (16, 8, 128)),
                        i16_sb[:, hf * 8:(hf + 1) * 8].rearrange(
                            "p (h o) -> p h o", o=1).broadcast_to((16, 8, 128)),
                        OP.mult),
                        waits=[(cTE, half_te)] if hf == 1 else [])
                    mskt_v = cV.n
                    for blk in range(2):
                        s0 = hf * 1024 + blk * 512
                        k.te(lambda e, s0=s0, blk=blk: e.matmul(
                            ps_x[:, s0:s0 + 512], o16_sb[:],
                            mskt[:, blk * 512:blk * 512 + 512],
                            start=True, stop=False),
                            waits=[(cV, mskt_v), (cSC, cSC.n)] if (hf == 0 and blk == 0)
                            else ([(cV, mskt_v)] if blk == 0 else []))
                        k.te(lambda e, s0=s0: e.matmul(
                            ps_x[:, s0:s0 + 512], ibf_sb[:], ust_sb[:, s0:s0 + 512],
                            start=False, stop=True), inc=(blk == 1))
                    half_te = k.cTE.n
                x_te = cTE.n
                k.v(lambda e: e.tensor_tensor(
                    gbuf[:].rearrange("p (h t) -> p h t", h=16),
                    ps_x[:].rearrange("p (h t) -> p h t", h=16),
                    bc_h(clb[:], 16, 128), OP.subtract), waits=[(cTE, x_te)])
                gs_v = cV.n
                dump("gin%d" % tile, gbuf[:], [(cV, gs_v)])
                k.sc(lambda e: e.activation(gbuf[:], gbuf[:], AF.Exp),
                     waits=[(cV, gs_v)] + ([(cDBG, cDBG.n)]
                                           if ("gin%d" % tile) in dbg else []))
                gexp_sc = cSC.n
                dump("gexp%d" % tile, gbuf[:], [(cSC, gexp_sc)])
                k.te(lambda e, i=i: e.matmul(
                    ps_b[:, 640:768], bcfm[:, 0, i * 128:(i + 1) * 128],
                    bcfm[:, 1, i * 128:(i + 1) * 128],
                    start=True, stop=True),
                     waits=[(cV, bcf_v), (cSC, sml_sc)], inc=True)
                cb_te = cTE.n
                k.v(lambda e: e.tensor_copy(cbsb[:], ps_b[:, 640:768]),
                    waits=[(cTE, cb_te)])
                cb_v = cV.n
                dump("cbsb%d" % tile, cbsb[:], [(cV, cb_v)])
                k.vdrain()
                k.v(lambda e: e.tensor_tensor(
                    gbuf[:].rearrange("p (h t) -> p h t", h=16),
                    gbuf[:].rearrange("p (h t) -> p h t", h=16),
                    bc_g(cbsb[:], 16, 128), OP.mult),
                    waits=[(cSC, gexp_sc), (cV, cb_v)])
                g_v = cV.n
                for h in range(16):
                    k.te(lambda e, h=h, i=i: e.matmul(
                        ps_a[:, h * 64:(h + 1) * 64],
                        gbuf[:, h * 128:(h + 1) * 128],
                        xtm[:, i, h * 64:(h + 1) * 64],
                        start=(h % 8 == 0), stop=(h % 8 == 7)),
                        waits=[(cV, g_v)] if h == 0 else [], inc=(h == 15))
                intra_te = cTE.n
                for h in range(16):
                    k.te(lambda e, h=h, i=i: e.matmul(
                        ps_b[:, h * 64:(h + 1) * 64],
                        bcfm[:, 1, i * 128:(i + 1) * 128],
                        state[:, h * 64:(h + 1) * 64],
                        start=(h % 8 == 0), stop=(h % 8 == 7)),
                        waits=[(cV, cb_v)] if h == 0 else [], inc=(h == 15))
                inter_te = cTE.n
                for h in range(16):
                    k.te(lambda e, h=h: e.matmul(
                        ps_x[:, h * 64:(h + 1) * 64], btm[:],
                        xw[:, h * 64:(h + 1) * 64],
                        start=(h % 8 == 0), stop=(h % 8 == 7)),
                        waits=[(cV, gs_v)] if h == 0 else [], inc=(h == 15))
                contrib_te = cTE.n
                k.v(lambda e: e.tensor_tensor(
                    scr4[:].rearrange("p (h d) -> p h d", h=16),
                    ps_b[:].rearrange("p (h d) -> p h d", h=16),
                    bc_h(fexp[:], 16, 64), OP.mult),
                    waits=[(cTE, inter_te)])
                k.v(lambda e: e.tensor_tensor(
                    xw[:].rearrange("p (h d) -> p h d", h=16),
                    state[:].rearrange("p (h d) -> p h d", h=16),
                    bc_h(lamb[:], 16, 64), OP.mult),
                    waits=[(cTE, contrib_te)])
                k.v(lambda e: e.tensor_tensor(ybuf[:], scr4[:, 0:1024], ps_a[:],
                                              OP.add), waits=[(cTE, intra_te)])
                k.v(lambda e, i=i: e.tensor_tensor(
                    gbuf[:, 1024:2048].rearrange("p (h d) -> p h d", h=16),
                    xtm[:, i, :].rearrange("p (h d) -> p h d", h=16),
                    bc_h(dvec_sb[:], 16, 64), OP.mult))
                k.v(lambda e: e.tensor_tensor(state[:], xw[:],
                                              ps_x[:, 0:1024], OP.add))
                k.v(lambda e: e.tensor_tensor(ybuf[:], ybuf[:],
                                              gbuf[:, 1024:2048], OP.add))
                y_v = cV.n
                dump("y%d" % tile, ybuf[:], [(cV, y_v)])
                dump("xtm%d" % tile, xtm[:, i, :], [(cV, y_v)])
                dump("btm%d" % tile, btm[:], [(cV, bcf_v)])

                k.vdrain()
                k.v(lambda e, i=i: e.tensor_tensor(gbuf[:, 0:1024], ybuf[:],
                                                   silz[:, i, :], OP.mult),
                    waits=[(cV, y_v)])
                yz_v = cV.n
                k.sc(lambda e: e.activation(TRASH[:, 0:1024], gbuf[:, 0:1024],
                                            AF.Square,
                                            accum_out=cols[:, C_SS:C_SS + 1]),
                     waits=[(cV, yz_v)])
                rs_sc = cSC.n
                k.v(lambda e: e.tensor_scalar(cols[:, C_VE:C_VE + 1],
                                              cols[:, C_SS:C_SS + 1], 1.0 / D, EPS,
                                              OP.mult, OP.add), waits=[(cSC, rs_sc)])
                rve_v = cV.n
                k.sc(lambda e: e.activation(cols[:, C_LNV:C_LNV + 1],
                                            cols[:, C_VE:C_VE + 1], AF.Ln),
                     waits=[(cV, rve_v)])
                k.scdrain()
                k.sc(lambda e: e.activation(cols[:, C_R:C_R + 1],
                                            cols[:, C_LNV:C_LNV + 1], AF.Exp,
                                            scale=-0.5))
                rr_sc = cSC.n
                k.v(lambda e: e.scalar_tensor_tensor(
                    scr1[:, 0:1024], gbuf[:, 0:1024], cols[:, C_R:C_R + 1],
                    nww_sb[:], OP.mult, OP.mult),
                    waits=[(cSC, rr_sc), (cTE, cTE.n)])
                yzn_v = cV.n
                dump("yzn%d" % tile, scr1[:, 0:1024], [(cV, yzn_v)])
                for gch in range(2):
                    for c4 in range(4):
                        cc = gch * 4 + c4
                        w = []
                        if cc == 0:
                            w = [(cV, yzn_v), (cSC, cSC.n)]
                        if cc == 4:
                            w = [(cSC, cSC.n)]
                        k.te(lambda e, cc=cc, c4=c4: e.transpose(
                            ps_x[:, 1024 + c4 * 128:1024 + (c4 + 1) * 128],
                            scr1[:, cc * 128:(cc + 1) * 128], ident_sb[:]),
                            waits=w, inc=(c4 == 3))
                    te_n = cTE.n
                    k.sc(lambda e, i=i, gch=gch: e.copy(
                        yzfm[:, i, gch * 4:(gch + 1) * 4, :].rearrange(
                            "p c t -> p (c t)"),
                        ps_x[:, 1024:1536]), waits=[(cTE, te_n)])
            scanC1_v, scanC1_sc = cV.n, cSC.n

            op_gate = {}
            for ec, (lo, hi) in enumerate(ech8):
                bufi = ec % 2
                k.dma("sync", "W%d" % bufi, lambda e, lo=lo, hi=hi, bufi=bufi:
                      e.dma_start(
                          out=strm[:, bufi, :, :],
                          in_=wo[:, :, lo:hi]),
                      waits=[(cTE, strm_gate[bufi])])
                wch = cW[bufi].n
                ps = ps_a if ec % 2 == 0 else ps_b
                for i in range(G):
                    for c in range(8):
                        waits = []
                        if i == 0 and c == 0:
                            waits = [(cW[bufi], wch), (cSC, scanC1_sc),
                                     (cST, prev_store_st)]
                            waits += op_gate.get(ec - 2, [(cV, scanC1_v)])
                        k.te(lambda e, i=i, c=c, ps=ps, bufi=bufi: e.matmul(
                            ps[:, i * 256:i * 256 + 128], yzfm[:, i, c, :],
                            strm[:, bufi, c, :], start=(c == 0), stop=(c == 7)),
                            waits=waits, inc=(i == G - 1 and c == 7))
                mm_te = cTE.n
                strm_gate[bufi] = mm_te
                ev = []
                for i in range(G):
                    k.v(lambda e, i=i, ps=ps, lo=lo: e.tensor_tensor(
                        u_st[:, i, lo:lo + 128], u_st[:, i, lo:lo + 128],
                        ps[:, i * 256:i * 256 + 128], OP.add),
                        waits=[(cTE, mm_te)])
                    ev.append((cV, cV.n))
                op_gate[ec] = ev
            x2_v = cV.n
            dump("x2_%d" % t0, u_st[:, 0, :], [(cV, x2_v)])

            for i in range(G):
                tile = t0 + i
                k.sc(lambda e, i=i: e.activation(
                    TRASH[:, 0:1024], u_st[:, i, :], AF.Identity,
                    accum_out=cols[:, C_SUM:C_SUM + 1]), waits=[(cV, x2_v)])
                k.sc(lambda e, i=i: e.activation(
                    TRASH[:, 0:1024], u_st[:, i, :], AF.Square,
                    accum_out=cols[:, C_SS:C_SS + 1]))
                st_sc = cSC.n
                k.v(lambda e: e.tensor_scalar(
                    cols[:, C_MSQ:C_MSQ + 1], cols[:, C_SUM:C_SUM + 1],
                    cols[:, C_SUM:C_SUM + 1], 1.0 / (D * D), OP.mult, OP.mult),
                    waits=[(cSC, st_sc)])
                k.v(lambda e: e.tensor_scalar(
                    cols[:, C_T:C_T + 1], cols[:, C_SS:C_SS + 1], 1.0 / D, EPS,
                    OP.mult, OP.add))
                k.v(lambda e: e.tensor_scalar(
                    cols[:, C_M:C_M + 1], cols[:, C_SUM:C_SUM + 1], 1.0 / D, None,
                    OP.mult))
                k.v(lambda e: e.tensor_tensor(
                    cols[:, C_VE:C_VE + 1], cols[:, C_T:C_T + 1],
                    cols[:, C_MSQ:C_MSQ + 1], OP.subtract))
                ve_v = cV.n
                k.sc(lambda e: e.activation(cols[:, C_LNV:C_LNV + 1],
                                            cols[:, C_VE:C_VE + 1], AF.Ln),
                     waits=[(cV, ve_v)])
                k.scdrain()
                k.sc(lambda e: e.activation(cols[:, C_R:C_R + 1],
                                            cols[:, C_LNV:C_LNV + 1], AF.Exp,
                                            scale=-0.5))
                r2_sc = cSC.n
                k.v(lambda e, i=i: e.tensor_scalar(
                    scr1[:, 0:1024], u_st[:, i, :], cols[:, C_M:C_M + 1], None,
                    OP.subtract), waits=[(cSC, r2_sc), (cTE, cTE.n)])
                k.vdrain()
                k.v(lambda e, i=i: e.tensor_tensor(u_st[:, i, :], u_st[:, i, :],
                                                   fbw[:], OP.add))
                k.v(lambda e: e.tensor_reduce(
                    bmx[:], scr1[:, 0:1024].rearrange("p (b kk) -> p b kk", kk=32),
                    AX, OP.max, apply_absolute_value=True))
                k.vdrain()
                k.v(lambda e: e.tensor_scalar(s8[:], bmx[:], 1.0 / 7.5, 1e-30,
                                              OP.mult, OP.max))
                k.vdrain()
                k.v(lambda e: e.tensor_scalar(rq8[:], s8[:], 0.125, None, OP.mult))
                rv = cV.n
                k.sc(lambda e: e.activation(rq8[:], rq8[:], AF.Ln), waits=[(cV, rv)])
                k.scdrain()
                k.sc(lambda e: e.activation(rq8[:], rq8[:], AF.Exp, scale=-1.0))
                rsc = cSC.n
                k.v(lambda e: e.tensor_scalar(s8[:], s8[:], cols[:, C_R:C_R + 1],
                                              0.125, OP.mult, OP.mult),
                    waits=[(cSC, rsc)])
                quant_core(scr1, gbuf, ybuf, TRASH, [(cV, cV.n)])
                xd_v = cV.n
                dump("xd%d" % tile, scr1[:, 0:1024], [(cV, xd_v)])
                for gch in range(2):
                    for c4 in range(4):
                        cc = gch * 4 + c4
                        w = []
                        if cc == 0:
                            w = [(cV, xd_v), (cSC, cSC.n)]
                        if cc == 4:
                            w = [(cSC, cSC.n)]
                        k.te(lambda e, cc=cc, c4=c4: e.transpose(
                            ps_x[:, 1024 + c4 * 128:1024 + (c4 + 1) * 128],
                            scr1[:, cc * 128:(cc + 1) * 128], ident_sb[:]),
                            waits=w, inc=(c4 == 3))
                    te_n = cTE.n
                    k.sc(lambda e, i=i, gch=gch: e.copy(
                        xdfm[:, i, gch * 4:(gch + 1) * 4, :].rearrange(
                            "p c t -> p (c t)"),
                        ps_x[:, 1024:1536]), waits=[(cTE, te_n)])
            qz_v, qz_sc = cV.n, cSC.n

            ff_gate = {}
            for ec, (lo, hi) in enumerate(ech8):
                bufi = ec % 2
                k.dma("sync", "W%d" % bufi, lambda e, lo=lo, hi=hi, bufi=bufi:
                      e.dma_start(
                          out=strm[:, bufi, :, :],
                          in_=wdqT[:, :, lo:hi]),
                      waits=[(cTE, strm_gate[bufi])])
                wch = cW[bufi].n
                ps = ps_a if ec % 2 == 0 else ps_b
                for i in range(G):
                    for c in range(8):
                        waits = []
                        if i == 0 and c == 0:
                            waits = [(cW[bufi], wch), (cSC, qz_sc)]
                            waits += ff_gate.get(ec - 2, [(cV, qz_v)])
                        k.te(lambda e, i=i, c=c, ps=ps, bufi=bufi: e.matmul(
                            ps[:, i * 256:i * 256 + 128], xdfm[:, i, c, :],
                            strm[:, bufi, c, :], start=(c == 0), stop=(c == 7)),
                            waits=waits, inc=(i == G - 1 and c == 7))
                mm_te = cTE.n
                strm_gate[bufi] = mm_te
                ev = []
                for i in range(G):
                    k.v(lambda e, i=i, ps=ps, lo=lo: e.tensor_tensor(
                        u_st[:, i, lo:lo + 128], u_st[:, i, lo:lo + 128],
                        ps[:, i * 256:i * 256 + 128], OP.add),
                        waits=[(cTE, mm_te)])
                    ev.append((cV, cV.n))
                ff_gate[ec] = ev
            ffn_v = cV.n
            for i in range(G):
                tile = t0 + i
                k.dma("gpsimd", "ST", lambda e, tile=tile, i=i: e.dma_start(
                    out=out_hbm[tile * PT:(tile + 1) * PT, :], in_=u_st[:, i, :]),
                    waits=[(cV, ffn_v)])
            prev_stageC = [(cV, cV.n), (cSC, cSC.n)]
            prev_store_st = cST.n

        k.add("gpsimd", lambda e: e.tensor_copy(cols[0:1, 0:1], cols[0:1, 1:2]),
              waits=[(cST, cST.n), (cDBG, cDBG.n), (cPRE, cPRE.n)])

        @block.tensor
        def _(eng):
            for fn in k.ops["tensor"]:
                fn(eng)

        @block.vector
        def _(eng):
            for fn in k.ops["vector"]:
                fn(eng)

        @block.scalar
        def _(eng):
            for fn in k.ops["scalar"]:
                fn(eng)

        @block.gpsimd
        def _(eng):
            for fn in k.ops["gpsimd"]:
                fn(eng)

        @block.sync
        def _(eng):
            for fn in k.ops["sync"]:
                fn(eng)

    return nc


def make_inputs(inp):
    import ml_dtypes
    w_in = np.asarray(inp["L0_attn_in_proj_w"], np.float32)
    w_h = np.asarray(inp["L0_attn_h_proj_w"], np.float32)
    wcat2 = np.concatenate([w_in, w_h[:, 0:1024]], axis=0).T
    wcat = np.ascontiguousarray(wcat2.reshape(8, 128, ETOT).transpose(1, 0, 2))
    whdt = np.ascontiguousarray(w_h[:, 1024:1056].T)
    wo = np.ascontiguousarray(np.asarray(inp["L0_attn_out_proj_w"], np.float32)
                              .T.reshape(8, 128, D).transpose(1, 0, 2))
    wffn = np.ascontiguousarray(np.asarray(inp["L0_ffn_weight"], np.float32))
    convw = np.asarray(inp["L0_attn_conv_w"], np.float32)
    wfm = np.ascontiguousarray(convw.reshape(10, 128, 4).transpose(1, 0, 2))
    cbfm = np.ascontiguousarray(
        np.asarray(inp["L0_attn_conv_b"], np.float32).reshape(10, 128).T)
    dbias = np.ascontiguousarray(np.broadcast_to(
        np.asarray(inp["L0_attn_dt_bias"], np.float32)[None, :], (128, 1)))
    gam = np.ascontiguousarray(np.broadcast_to(
        np.asarray(inp["L0_attn_gamma"], np.float32)[None, :], (128, 16)))
    alog = np.ascontiguousarray(np.broadcast_to(
        np.asarray(inp["L0_attn_A_log"], np.float32)[None, :], (128, 16)))
    dvec = np.ascontiguousarray(np.broadcast_to(
        np.asarray(inp["L0_attn_D"], np.float32)[None, :], (128, 16)))
    nww = np.ascontiguousarray(np.broadcast_to(
        np.asarray(inp["L0_attn_norm_w"], np.float32)[None, :], (128, 1024)))
    fbias = np.ascontiguousarray(
        np.asarray(inp["L0_ffn_bias"], np.float32)[None, :])
    sg = np.arange(128)
    tg = np.arange(128)
    ustf = np.repeat(
        (-BIGM * (sg[:, None] > tg[None, :]).astype(np.float32))[:, None, :],
        16, axis=1).reshape(128, 2048)
    consts = dict(
        ident=np.eye(128, dtype=np.float32),
        cumm=(sg[:, None] <= tg[None, :]).astype(np.float32),
        ust=ustf.astype(ml_dtypes.bfloat16),
        ibf=np.eye(128, dtype=np.float32).astype(ml_dtypes.bfloat16),
        i16=np.eye(16, dtype=np.float32),
        ones16=np.ones((16, 128), np.float32),
        ones1=np.ones((1, 128), np.float32),
        ones128=np.ones((128, 128), np.float32),
        ninvt=np.ascontiguousarray(
            (-1.0 / (np.arange(T, dtype=np.float64) + 1.0))
            .astype(np.float32).reshape(NTILE, 128).T),
    )
    u_full = np.asarray(inp["L0_attn_u"], np.float32)
    shared = dict(wcat=wcat, wo=wo, wffn=wffn, whdt=whdt, wfm=wfm, cbfm=cbfm,
                  dbias=dbias, gam=gam, alog=alog, dvec=dvec, nww=nww,
                  fbias=fbias, **consts)
    in_maps = []
    for core in range(8):
        m = dict(shared)
        m["u"] = np.ascontiguousarray(u_full[core])
        in_maps.append(m)
    return in_maps


def kernel(**inputs):
    in_maps = make_inputs(inputs)
    nc = build_core()
    res = run_bass_kernel_spmd(nc, in_maps, core_ids=list(range(8)))
    out = np.stack([np.asarray(r["out"]) for r in res.results], axis=0)
    return out.astype(np.float32)
